# version 02
# baseline (speedup 1.0000x reference)
import numpy as np
from contextlib import ExitStack

import concourse.bass as bass
import concourse.library_config as libcfg
import concourse.mybir as mybir
from concourse.bass_utils import run_bass_kernel_spmd

F32 = mybir.dt.float32
BF16 = mybir.dt.bfloat16
I16 = mybir.dt.int16
I32 = mybir.dt.int32
AF = mybir.ActivationFunctionType
ALU = mybir.AluOpType
AX = mybir.AxisListType
IOA = bass.IndirectOffsetOnAxis

T = 2048
D = 1024
NT = T // 128
ND = D // 128
HEADS = 8
WIN = 64
NWIN = T // WIN
E = 8
CAP0 = 512
CAP1 = 1024
H = 4096
NH = H // 128
EPS = 1e-5
ISQ = float(1.0 / np.sqrt(128.0))

STAGE = 4
DBG_MOE = False
DBG_DUMP = None


class KB:

    def __init__(self, nc, ctx):
        self.nc, self.ctx = nc, ctx
        self.streams = {n: [] for n in ("sync", "vector", "scalar", "tensor", "gpsimd")}
        self.semh, self.cnt = {}, {}

    def sem(self, *names):
        for n in names:
            self.semh[n] = self.ctx.enter_context(self.nc.semaphore(n))
            self.cnt[n] = 0

    def emit(self, eng, fn, inc=None, by=1):
        tgt = None
        if inc is not None:
            self.cnt[inc] += by
            tgt = self.cnt[inc]
            h = self.semh[inc]
            self.streams[eng].append(lambda E, fn=fn, h=h, by=by: fn(E).then_inc(h, by))
        else:
            self.streams[eng].append(fn)
        return tgt

    def wait(self, eng, sem, tgt):
        if tgt is None or tgt <= 0:
            return
        h = self.semh[sem]
        self.streams[eng].append(lambda E, h=h, t=tgt: E.wait_ge(h, t))

    def val(self, sem):
        return self.cnt[sem]


def build_core_program():
    nc = bass.Bass("TRN2", target_bir_lowering=False, detect_race_conditions=False)

    def din(name, shape):
        return nc.dram_tensor(name, shape, F32, kind="ExternalInput").ap()

    x_in = din("x", [T, D])
    w_in = din("in_proj_w", [3 * D, D])
    b_in = din("in_proj_b", [3 * D])
    w_out = din("out_w", [D, D])
    b_out = din("out_b", [D])
    gate_w = din("gate_w", [D, E])
    w1 = din("w1", [E, D, H])
    w2 = din("w2", [E, H, D])
    A_m = din("A", [128, 128])
    Bm_m = din("Bm", [D, 128])
    Wctx = din("W_ctx", [D, 128])
    sh_up = din("shared_up", [D, H])
    sh_dn = din("shared_down", [H, D])
    r_up = din("routed_up", [E, D, H])
    r_dn = din("routed_down", [E, H, D])
    rt_w = din("router_w", [D, E])

    out_t = nc.dram_tensor("out", [T, D], F32, kind="ExternalOutput").ap()
    if DBG_MOE:
        dbg_i = nc.dram_tensor("dbg_i", [3, 128, 64], F32, kind="ExternalOutput").ap()
        dbg_l = nc.dram_tensor("dbg_l", [4, 8, T], F32, kind="ExternalOutput").ap()
    moe_src = nc.dram_tensor("moe_src", [T, D], F32, kind="Internal").ap()
    winT_d = nc.dram_tensor("winT_d", [24, 128, ND * 128], BF16, kind="Internal").ap()
    moe_out = nc.dram_tensor("moe_out", [T, D], F32, kind="Internal").ap()

    with ExitStack() as ctx:
        kb = KB(nc, ctx)
        sb = lambda name, shape, dt: ctx.enter_context(nc.sbuf_tensor(name, shape, dt))
        ps = lambda name, shape: ctx.enter_context(nc.psum_tensor(name, shape, F32))
        kb.sem("cV", "cS", "cT", "cG",
               "sA", "sX", "sWA", "sWB", "sSA", "sSB", "sLA", "sLB", "sO",
               "sMS", "sZ", "sSP", "sGA", "sGB", "sSC", "sYA", "sYB",
               "sW2A", "sW2B", "sRA", "sRB", "sL1")
        ALL_SEMS = list(kb.semh.keys())

        def barrier():
            snap = {s: kb.val(s) for s in ALL_SEMS}
            for eng in ("vector", "scalar", "tensor", "gpsimd", "sync"):
                for s, v in snap.items():
                    if v > 0:
                        kb.wait(eng, s, v)

        stream = sb("stream", [128, NT * D], F32)
        xfm = sb("xfm", [128, ND * T], BF16)
        zt = sb("zt", [128, 512], F32)
        id128f = sb("id128f", [128, 128], F32)
        id128b = sb("id128b", [128, 128], BF16)
        id8f = sb("id8f", [8, 8], F32)
        id8b = sb("id8b", [8, 8], BF16)
        ones1 = sb("ones1", [1, 128], F32)
        st6 = sb("st6", [128, 12], F32)
        st2 = sb("st2", [128, 2], F32)
        mrs = sb("mrs", [128, 2], F32)
        wstg = sb("wstg", [128, 2 * 1024], F32)
        gidx = sb("gidx", [128, 64], I32)
        sidx = sb("sidx", [128, 64], I32)
        gcol = sb("gcol", [128, 64], F32)
        iota16 = sb("iota16", [128, T], I16)
        xf32 = sb("xf32", [128, 2 * 128], F32)

        pb = [ps(f"pb{i}", [128, 512]) for i in range(8)]

        V, SC, TE, G, SY = "vector", "scalar", "tensor", "gpsimd", "sync"

        _breg = {}

        def breg_of(E_):
            if "r" not in _breg:
                _breg["r"] = E_.to_reg(T - 1)
            return _breg["r"]

        def dma(eng_q, semn, out_ap, in_ap, slow=False):
            return kb.emit(eng_q, lambda E, o=out_ap, i=in_ap, s=slow: E.dma_start(
                out=o, in_=i, allow_slow_non_contiguous=s), inc=semn, by=16)

        def phase_init():
            kb.emit(V, lambda E: E.memset(zt[:], 0.0))
            kb.emit(V, lambda E: E.memset(ones1[:], 1.0))
            kb.emit(V, lambda E: E.memset(id128f[:], 1.0))
            kb.emit(V, lambda E: E.memset(id8f[:], 1.0))
            kb.emit(V, lambda E: E.memset(id8b[:], 1.0))
            t_ms = kb.emit(V, lambda E: E.memset(id128b[:], 1.0), inc="cV")
            kb.wait(G, "cV", t_ms)
            kb.emit(G, lambda E: E.affine_select(out=id128f[:], in_=id128f[:],
                    pattern=[[-1, 128]], compare_op=ALU.is_equal, fill=0.0,
                    channel_multiplier=1))
            kb.emit(G, lambda E: E.affine_select(out=id128b[:], in_=id128b[:],
                    pattern=[[-1, 128]], compare_op=ALU.is_equal, fill=0.0,
                    channel_multiplier=1))
            kb.emit(G, lambda E: E.affine_select(out=id8f[:], in_=id8f[:],
                    pattern=[[-1, 8]], compare_op=ALU.is_equal, fill=0.0,
                    channel_multiplier=1))
            kb.emit(G, lambda E: E.affine_select(out=id8b[:], in_=id8b[:],
                    pattern=[[-1, 8]], compare_op=ALU.is_equal, fill=0.0,
                    channel_multiplier=1))
            t_id = kb.emit(G, lambda E: E.iota(iota16[:], pattern=[[1, T]], base=1,
                    channel_multiplier=0, allow_small_or_imprecise_dtypes=True), inc="cG")
            t_b = dma(SY, "sA", bqkv[:], b_in.rearrange("(a p) -> p a", p=128), slow=True)
            t_ob = dma(SY, "sA", wstg[0:1, 1024:2048], b_out.rearrange("(o d) -> o d", o=1))
            kb.wait(TE, "sA", 32)
            kb.wait(TE, "cV", t_ms)
            t_rep = None
            for n in range(2):
                t_rep = kb.emit(TE, lambda E, n=n: E.matmul(
                    pb[6 + n][:, 0:512], ones1[:],
                    wstg[0:1, 1024 + 512 * n:1024 + 512 * (n + 1)],
                    start=True, stop=True), inc="cT")
            kb.wait(V, "cT", t_rep)
            kb.emit(V, lambda E: E.tensor_scalar_mul(outb2[:, 0:512], pb[6][:, 0:512], 2.0))
            t_ob2 = kb.emit(V, lambda E: E.tensor_scalar_mul(outb2[:, 512:1024], pb[7][:, 0:512], 2.0), inc="cV")
            return {"bqkv": t_b, "ob2": t_ob2, "ids": t_id}

        def phase_input(init_toks):
            kb.wait(TE, "cG", init_toks["ids"])
            for j in range(NT):
                q = SY if j % 2 == 0 else SC
                dma(q, "sX", stream[:, D * j:D * (j + 1)],
                    x_in[128 * j:128 * (j + 1), :])
            return build_xfm(gate_dma=("sX", 16 * NT))[0]

        def build_xfm(gate_dma=None, gate_all=None, gate_mm=False, gate_tok=None):
            banks = [pb[0], pb[1], pb[2], pb[3]]
            evac_tok = [None] * 4
            i = 0
            if gate_all is not None:
                kb.wait(TE, "cV", gate_all)
            if gate_dma is not None:
                kb.wait(TE, gate_dma[0], gate_dma[1])
            if gate_mm:
                kb.wait(TE, "sL1", gate_tok)
            t_gmm = None
            gmm_tok = [None, None]
            for j in range(NT):
                for k in range(ND):
                    b = banks[i % 4]
                    if evac_tok[i % 4] is not None:
                        kb.wait(TE, "cV", evac_tok[i % 4])
                    tt = kb.emit(TE, lambda E, j=j, k=k, b=b: E.transpose(
                        b[:, 0:128], stream[:, D * j + 128 * k:D * j + 128 * (k + 1)],
                        id128f[:]), inc="cT")
                    kb.wait(V, "cT", tt)
                    if gate_mm:
                        sl = i % 2
                        if gmm_tok[sl] is not None:
                            kb.wait(V, "cT", gmm_tok[sl])
                        kb.emit(V, lambda E, j=j, k=k, b=b, sl=sl: E.tensor_copy(
                            out=xf32[:, 128 * sl:128 * (sl + 1)], in_=b[:, 0:128]))
                    evac_tok[i % 4] = kb.emit(V, lambda E, j=j, k=k, b=b: E.tensor_copy(
                        out=xfm[:, T * k + 128 * j:T * k + 128 * (j + 1)],
                        in_=b[:, 0:128]), inc="cV")
                    if gate_mm:
                        kb.wait(TE, "cV", evac_tok[i % 4])
                        sl = i % 2
                        t_gmm = kb.emit(TE, lambda E, j=j, k=k, sl=sl: E.matmul(
                            pb[5][:, 8 * j:8 * (j + 1)],
                            xf32[:, 128 * sl:128 * (sl + 1)],
                            wstg[:, 8 * k:8 * (k + 1)],
                            start=(j == 0 and k == 0), stop=(j == NT - 1 and k == ND - 1)),
                            inc="cT")
                        gmm_tok[sl] = t_gmm
                    i += 1
            return kb.val("cV"), t_gmm

        def ln_all():
            for j in range(NT):
                sl = stream[:, D * j:D * (j + 1)]
                kb.emit(V, lambda E, sl=sl: E.bn_stats(st6[:, 0:6], sl[:, 0:512]))
                tbs = kb.emit(V, lambda E, sl=sl: E.bn_stats(st6[:, 6:12], sl[:, 512:1024]), inc="cV")
                kb.wait(V, "cV", tbs)
                tba = kb.emit(V, lambda E: E.bn_aggr(st2[:], st6[:]), inc="cV")
                kb.wait(V, "cV", tba)
                tv = kb.emit(V, lambda E: E.tensor_scalar_add(st2[:, 1:2], st2[:, 1:2], float(EPS)), inc="cV")
                kb.wait(SC, "cV", tv)
                tsc = kb.emit(SC, lambda E: E.activation(out=mrs[:, 0:1], in_=st2[:, 1:2],
                        func=AF.Sqrt, bias=0.0, scale=1.0), inc="cS")
                kb.wait(V, "cS", tsc)
                trc = kb.emit(V, lambda E: E.reciprocal(out=mrs[:, 1:2], in_=mrs[:, 0:1]), inc="cV")
                kb.wait(V, "cV", trc)
                kb.emit(V, lambda E, sl=sl: E.tensor_scalar(
                    out=sl, in0=sl, scalar1=st2[:, 0:1], scalar2=mrs[:, 1:2],
                    op0=ALU.subtract, op1=ALU.mult), inc="cV")
            return kb.val("cV")


        def moe_selection(wg_dram, is_l0, cap, t_gmm, sel):
            GG = cap // 128
            kb.wait(V, "cT", t_gmm)
            probs = sel["probs"]
            p3 = probs[:].rearrange("p (g e) -> p g e", e=8)
            kb.emit(V, lambda E: E.tensor_copy(out=probs[:], in_=pb[5][:, 0:128]))
            tmx = kb.emit(V, lambda E: E.reduce_max(out=sel["pmax"][:], in_=p3, axis=AX.X), inc="cV")
            kb.wait(V, "cV", tmx)
            pmaxb = sel["pmax"][:].rearrange("p (g o) -> p g o", o=1).to_broadcast([128, NT, 8])
            tsb = kb.emit(V, lambda E: E.tensor_tensor(out=p3, in0=p3, in1=pmaxb, op=ALU.subtract), inc="cV")
            kb.wait(SC, "cV", tsb)
            tex = kb.emit(SC, lambda E: E.activation(out=probs[:], in_=probs[:], func=AF.Exp,
                    bias=0.0, scale=1.0), inc="cS")
            kb.wait(V, "cS", tex)
            tsm = kb.emit(V, lambda E: E.reduce_sum(out=sel["psum_r"][:], in_=p3, axis=AX.X), inc="cV")
            kb.wait(V, "cV", tsm)
            trc = kb.emit(V, lambda E: E.reciprocal(out=sel["psum_r"][:], in_=sel["psum_r"][:]), inc="cV")
            kb.wait(V, "cV", trc)
            prb = sel["psum_r"][:].rearrange("p (g o) -> p g o", o=1).to_broadcast([128, NT, 8])
            tpd = kb.emit(V, lambda E: E.tensor_tensor(out=p3, in0=p3, in1=prb, op=ALU.mult), inc="cV")

            if is_l0:
                kb.wait(V, "cV", tpd)
                tpc = None
                for e in range(E):
                    tpc = kb.emit(V, lambda E_, e=e: E_.tensor_copy(
                        out=sel["mx8s"][:, 16 * e:16 * (e + 1)], in_=probs[:, e::8]),
                        inc="cV" if e == E - 1 else None)
                kb.wait(G, "cV", tpc)
                kb.emit(G, lambda E: E.load_library(libcfg.attn))
                tkth = None
                for e in range(E):
                    tkth = kb.emit(G, lambda E_, e=e: E_.kth_largest(
                        sel["thr"][0:1, 2 * e:2 * e + 2], sel["mx8s"][:, 16 * e:16 * (e + 1)],
                        n_per_lane=16, k=510, quantile=0.5), inc="cG")
                kb.wait(TE, "cG", tkth)
                trp = kb.emit(TE, lambda E: E.matmul(
                    pb[1][0:128, 0:8], ones1[:], sel["thr"][0:1, 1::2],
                    start=True, stop=True), inc="cT")
                kb.wait(V, "cT", trp)
                tre = kb.emit(V, lambda E: E.tensor_copy(out=sel["thr_rep"][:], in_=pb[1][0:128, 0:8]), inc="cV")
                kb.wait(V, "cV", tre)
                thb = sel["thr_rep"][:].rearrange("p (o e) -> p o e", o=1).to_broadcast([128, NT, 8])
                tmk = kb.emit(V, lambda E: E.tensor_tensor(
                    out=sel["maskb"][:].rearrange("p (g e) -> p g e", e=8),
                    in0=p3, in1=thb, op=ALU.is_ge), inc="cV")
            else:
                tm8 = None
                for g in range(NT):
                    tm8 = kb.emit(V, lambda E, g=g: E.max(
                        out=sel["mx8s"][:, 8 * g:8 * (g + 1)], in_=probs[:, 8 * g:8 * (g + 1)]),
                        inc="cV" if g == NT - 1 else None)
                kb.wait(V, "cV", tm8)
                t2b = sel["mx8s"][:, 1::8].rearrange("p (g o) -> p g o", o=1).to_broadcast([128, NT, 8])
                tmk = kb.emit(V, lambda E: E.tensor_tensor(
                    out=sel["maskb"][:].rearrange("p (g e) -> p g e", e=8),
                    in0=p3, in1=t2b, op=ALU.is_ge), inc="cV")
            kb.wait(V, "cV", tmk)
            tgt_ = kb.emit(V, lambda E: E.tensor_tensor(out=sel["gatesb"][:], in0=probs[:],
                    in1=sel["maskb"][:], op=ALU.mult), inc="cV")
            kb.wait(TE, "cV", tgt_)

            tml = tgl = None
            for g in range(NT):
                tt = kb.emit(TE, lambda E, g=g: E.transpose(
                    pb[2][0:8, 0:128], sel["maskb"][:, 8 * g:8 * (g + 1)], id128f[:]), inc="cT")
                kb.wait(V, "cT", tt)
                tml = kb.emit(V, lambda E, g=g: E.tensor_copy(
                    out=sel["m_em"][0:8, 128 * g:128 * (g + 1)], in_=pb[2][0:8, 0:128]), inc="cV")
                kb.wait(TE, "cV", tml)
                tt2 = kb.emit(TE, lambda E, g=g: E.transpose(
                    pb[3][0:8, 0:128], sel["gatesb"][:, 8 * g:8 * (g + 1)], id128f[:]), inc="cT")
                kb.wait(V, "cT", tt2)
                tgl = kb.emit(V, lambda E, g=g: E.tensor_copy(
                    out=sel["g_em"][0:8, 128 * g:128 * (g + 1)], in_=pb[3][0:8, 0:128]), inc="cV")
                kb.wait(TE, "cV", tgl)

            tps = kb.emit(V, lambda E: E.tensor_tensor_scan(
                out=sel["pos_em"][0:8, :], data0=sel["m_em"][0:8, :], data1=sel["m_em"][0:8, :],
                initial=0.0, op0=ALU.add, op1=ALU.bypass), inc="cV")
            kb.wait(V, "cV", tps)
            kb.emit(V, lambda E: E.tensor_tensor(out=sel["pos_em"][0:8, :], in0=sel["pos_em"][0:8, :],
                    in1=sel["m_em"][0:8, :], op=ALU.mult))
            kb.emit(V, lambda E: E.tensor_scalar_add(sel["pos_em"][0:8, :], sel["pos_em"][0:8, :], -1.0))
            kb.emit(V, lambda E: E.tensor_single_scalar(out=sel["m_em"][0:8, :],
                    in_=sel["pos_em"][0:8, :], scalar=1536.0, op=ALU.is_ge))
            kb.emit(V, lambda E: E.scalar_tensor_tensor(out=sel["pos_em"][0:8, :],
                    in0=sel["m_em"][0:8, :], scalar=-10000.0, in1=sel["pos_em"][0:8, :],
                    op0=ALU.mult, op1=ALU.add))
            kb.emit(V, lambda E: E.tensor_copy(out=sel["pos16"][0:8, :], in_=sel["pos_em"][0:8, :]))
            kb.emit(V, lambda E: E.memset(sel["tgt16"][:, :], -1))
            tms_ = kb.emit(V, lambda E: E.memset(sel["gsp"][:, :], 0.0), inc="cV")
            kb.wait(SY, "cV", tms_)
            dma(SY, "sSP", sel["tgt16"][0:128:16, :], sel["pos16"][0:8, :])
            tsp = dma(SY, "sSP", sel["gsp"][0:128:16, :], sel["g_em"][0:8, :])
            kb.wait(G, "sSP", tsp)

            kb.emit(G, lambda E: E.load_library(libcfg.local_scatter))
            kb.emit(G, lambda E: E.local_scatter(out_ap=sel["lst16"][:, 0:1536], data_ap=iota16[:],
                    idxs_ap=sel["tgt16"][:], channels=128, num_elems=1536, num_idxs=T))
            tls = kb.emit(G, lambda E: E.local_scatter(out_ap=sel["glst"][:, 0:1536], data_ap=sel["gsp"][:],
                    idxs_ap=sel["tgt16"][:], channels=128, num_elems=1536, num_idxs=T), inc="cG")
            kb.wait(SY, "cG", tls)
            dma(SY, "sSP", sel["lstC16"][0:8, 0:cap], sel["lst16"][0:128:16, 0:cap])
            tcp = dma(SY, "sSP", sel["glstC"][0:8, 0:cap], sel["glst"][0:128:16, 0:cap])
            kb.wait(V, "sSP", tcp)
            tlf = kb.emit(V, lambda E: E.tensor_copy(out=sel["lstC"][0:8, 0:cap],
                    in_=sel["lstC16"][0:8, 0:cap]), inc="cV")
            kb.wait(TE, "cV", tlf)
            if DBG_MOE:
                kb.emit(V, lambda E: E.tensor_copy(out=sel["pos_em"][0:8, 0:cap],
                        in_=sel["glstC"][0:8, 0:cap]))
                tdl = kb.emit(V, lambda E: E.tensor_copy(out=sel["m_em"][0:8, 0:T],
                        in_=sel["lst16"][0:128:16, 0:T]) if False else
                        E.tensor_copy(out=sel["m_em"][0:8, 0:1], in_=sel["lstC"][0:8, 0:1]), inc="cV")
                kb.wait(SC, "cV", tdl)
                dma(SC, "sO", dbg_l[0, :, 0:CAP1], sel["lstC"][0:8, 0:CAP1])
                dma(SC, "sO", dbg_l[1, :, 0:CAP1], sel["pos_em"][0:8, 0:CAP1])


            for gg in range(GG):
                tt = kb.emit(TE, lambda E, gg=gg: E.matmul(
                    pb[4][0:128, 0:8], sel["lstC"][0:8, 128 * gg:128 * (gg + 1)], id8f[:],
                    start=True, stop=True), inc="cT")
                kb.wait(V, "cT", tt)
                tl1 = kb.emit(V, lambda E: E.tensor_copy(out=sel["lcolf"][:], in_=pb[4][0:128, 0:8]), inc="cV")
                kb.wait(TE, "cV", tl1)
                tt2 = kb.emit(TE, lambda E, gg=gg: E.matmul(
                    pb[5][0:128, 0:8], sel["glstC"][0:8, 128 * gg:128 * (gg + 1)], id8b[:],
                    start=True, stop=True), inc="cT")
                kb.wait(V, "cT", tt2)
                tg1 = kb.emit(V, lambda E, gg=gg: E.tensor_copy(
                    out=gcol[:, 8 * gg:8 * (gg + 1)], in_=pb[5][0:128, 0:8]), inc="cV")
                kb.wait(TE, "cV", tg1)
                if DBG_MOE and gg == 0:
                    kb.emit(V, lambda E: E.tensor_copy(out=sel["pmax"][:, 0:8], in_=sel["lcolf"][:]))
                kb.emit(V, lambda E: E.tensor_single_scalar(out=sel["iseq"][:], in_=sel["lcolf"][:],
                        scalar=0.0, op=ALU.is_equal))
                te1 = kb.emit(V, lambda E: E.tensor_scalar_add(sel["t1f"][:], sel["lcolf"][:], -1.0), inc="cV")
                kb.wait(V, "cV", te1)
                te2 = kb.emit(V, lambda E: E.tensor_add(
                    out=sel["t2f"][:], in0=sel["t1f"][:], in1=sel["iseq"][:]), inc="cV")
                kb.wait(V, "cV", te2)
                if DBG_MOE and gg == 0:
                    kb.emit(V, lambda E: E.tensor_copy(out=sel["pmax"][:, 8:16], in_=sel["t2f"][:]))
                kb.emit(V, lambda E, gg=gg: E.tensor_copy(
                    out=gidx[:, 8 * gg:8 * (gg + 1)], in_=sel["t2f"][:]))
                te3 = kb.emit(V, lambda E: E.scalar_tensor_tensor(out=sel["lcolf"][:], in0=sel["iseq"][:],
                        scalar=10000.0, in1=sel["t1f"][:], op0=ALU.mult, op1=ALU.add), inc="cV")
                kb.wait(V, "cV", te3)
                kb.emit(V, lambda E, gg=gg: E.tensor_copy(
                    out=sidx[:, 8 * gg:8 * (gg + 1)], in_=sel["lcolf"][:]), inc="cV")
                if DBG_MOE and gg == 0:
                    tdmp = kb.val("cV")
                    kb.wait(SC, "cV", tdmp)
                    tso = dma(SC, "sO", dbg_l[2].rearrange("r c -> (r c)")[0:2048],
                              sel["pmax"][0:128, 0:16])
                    kb.wait(V, "sO", tso)

        def moe_zero_out():
            tz = None
            for j in range(NT):
                for n in range(2):
                    tz = dma(SC, "sZ", moe_out[128 * j:128 * (j + 1), 512 * n:512 * (n + 1)], zt[:])
            return tz

        def moe_experts(wup, wdn, cap, ex, t_zero):
            GG = cap // 128
            NWV = (GG + 3) // 4
            xg_free = [None, None]
            w1_ld = [None, None]
            ysc_tok = [None, None]
            w1bf_free = None
            scat_prev = kb.val("sSC")
            kb.wait(G, "sZ", t_zero)
            kb.wait(G, "sMS", kb.val("sMS"))
            for e in range(E):
                for wv in range(NWV):
                    g0 = 4 * wv
                    nt_w = min(4, GG - g0)
                    for gi in range(nt_w):
                        gg = g0 + gi
                        sl = gg % 2
                        if xg_free[sl] is not None:
                            kb.wait(G, "cV", xg_free[sl])
                        gt = kb.emit(G, lambda E_, e=e, gg=gg, sl=sl: E_.indirect_dma_start(
                            out=ex["xg_tm"][:, 1024 * sl:1024 * (sl + 1)], out_offset=None,
                            in_=moe_src, in_offset=IOA(gidx[:, 8 * gg + e:8 * gg + e + 1], 0),
                            bounds_check=breg_of(E_), oob_is_err=False),
                            inc="sGA" if sl == 0 else "sGB", by=16)
                        kb.wait(TE, "sGA" if sl == 0 else "sGB", gt)
                        tvx = None
                        for k in range(ND):
                            bq_ = pb[k % 2]
                            tt = kb.emit(TE, lambda E_, k=k, sl=sl, b=bq_: E_.transpose(
                                b[:, 0:128], ex["xg_tm"][:, 1024 * sl + 128 * k:1024 * sl + 128 * (k + 1)],
                                id128f[:]), inc="cT")
                            kb.wait(V, "cT", tt)
                            tvx = kb.emit(V, lambda E_, k=k, gi=gi, b=bq_: E_.tensor_copy(
                                out=ex["xg_fm"][:, 512 * k + 128 * gi:512 * k + 128 * (gi + 1)],
                                in_=b[:, 0:128]), inc="cV")
                            kb.wait(TE, "cV", tvx)
                        xg_free[sl] = tvx

                    kb.wait(SY, "cV", kb.val("cV"))
                    kb.wait(SY, "sSC", scat_prev)
                    for half in range(2):
                        if w1bf_free is not None:
                            kb.wait(V, "cT", w1bf_free)
                        lastc = None
                        for k in range(ND):
                            hh = k % 2
                            if w1_ld[hh] is not None:
                                kb.wait(SY, "cV", w1_ld[hh])
                            dma(SY, "sWA" if hh == 0 else "sWB",
                                wstg[:, 1024 * hh:1024 * (hh + 1)],
                                wup[e, 128 * k:128 * (k + 1), 2048 * half:2048 * half + 1024])
                            t_w2 = dma(SY, "sWA" if hh == 0 else "sWB",
                                       ex["w1stg2"][:, 1024 * hh:1024 * (hh + 1)],
                                       wup[e, 128 * k:128 * (k + 1), 2048 * half + 1024:2048 * half + 2048])
                            kb.wait(V, "sWA" if hh == 0 else "sWB", t_w2)
                            kb.emit(V, lambda E_, k=k, hh=hh: E_.tensor_copy(
                                out=ex["w1bf"][:, 2048 * k:2048 * k + 1024],
                                in_=wstg[:, 1024 * hh:1024 * (hh + 1)]))
                            lastc = kb.emit(V, lambda E_, k=k, hh=hh: E_.tensor_copy(
                                out=ex["w1bf"][:, 2048 * k + 1024:2048 * k + 2048],
                                in_=ex["w1stg2"][:, 1024 * hh:1024 * (hh + 1)]), inc="cV")
                            w1_ld[hh] = lastc
                        kb.wait(TE, "cV", lastc)
                        for hc in range(16):
                            b = pb[2 + hc % 2]
                            tmu = None
                            for k in range(ND):
                                tmu = kb.emit(TE, lambda E_, hc=hc, k=k, b=b: E_.matmul(
                                    b[:, 0:512],
                                    ex["w1bf"][:, 2048 * k + 128 * hc:2048 * k + 128 * (hc + 1)],
                                    ex["xg_fm"][:, 512 * k:512 * (k + 1)],
                                    start=(k == 0), stop=(k == ND - 1)),
                                    inc="cT" if k == ND - 1 else None)
                            kb.wait(SC, "cT", tmu)
                            hcg = 16 * half + hc
                            tge = kb.emit(SC, lambda E_, hcg=hcg, b=b: E_.activation(
                                out=ex["hbuf"][:, 512 * hcg:512 * (hcg + 1)], in_=b[:, 0:512],
                                func=AF.Gelu, bias=0.0, scale=1.0), inc="cS")
                            kb.wait(TE, "cS", tge)
                        w1bf_free = kb.val("cT")

                    kb.wait(TE, "cS", kb.val("cS"))
                    kb.wait(TE, "cV", kb.val("cV"))
                    kb.wait(SY, "cV", kb.val("cV"))
                    w2_ld = [None, None]
                    w2_mm = [None, None]
                    tdn = None
                    for hc in range(NH):
                        hh = hc % 2
                        if w2_ld[hh] is not None:
                            kb.wait(SY, "cV", w2_ld[hh])
                        t_w = dma(SY, "sW2A" if hh == 0 else "sW2B",
                                  wstg[:, 1024 * hh:1024 * (hh + 1)],
                                  wdn[e, 128 * hc:128 * (hc + 1), :])
                        kb.wait(V, "sW2A" if hh == 0 else "sW2B", t_w)
                        if w2_mm[hh] is not None:
                            kb.wait(V, "cT", w2_mm[hh])
                        tcw = kb.emit(V, lambda E_, hh=hh: E_.tensor_copy(
                            out=ex["w2bf"][:, 1024 * hh:1024 * (hh + 1)],
                            in_=wstg[:, 1024 * hh:1024 * (hh + 1)]), inc="cV")
                        w2_ld[hh] = tcw
                        kb.wait(TE, "cV", tcw)
                        for m in range(nt_w):
                            for n in range(2):
                                b = pb[m * 2 + n]
                                tdn = kb.emit(TE, lambda E_, hc=hc, m=m, n=n, hh=hh, b=b: E_.matmul(
                                    b[:, 0:512],
                                    ex["hbuf"][:, 512 * hc + 128 * m:512 * hc + 128 * (m + 1)],
                                    ex["w2bf"][:, 1024 * hh + 512 * n:1024 * hh + 512 * (n + 1)],
                                    start=(hc == 0), stop=(hc == NH - 1)),
                                    inc="cT" if (m == nt_w - 1 and n == 1) else None)
                        w2_mm[hh] = tdn
                    kb.wait(V, "cT", tdn)
                    kb.wait(G, "sSC", scat_prev)
                    for m in range(nt_w):
                        gg = g0 + m
                        sl = m % 2
                        if ysc_tok[sl] is not None:
                            kb.wait(V, "sSC", ysc_tok[sl])
                        tyv = None
                        for n in range(2):
                            b = pb[m * 2 + n]
                            tyv = kb.emit(V, lambda E_, n=n, e=e, gg=gg, sl=sl, b=b: E_.tensor_scalar(
                                out=ex["ybuf"][:, 1024 * sl + 512 * n:1024 * sl + 512 * (n + 1)],
                                in0=b[:, 0:512], scalar1=gcol[:, 8 * gg + e:8 * gg + e + 1],
                                scalar2=None, op0=ALU.mult), inc="cV")
                        kb.wait(G, "cV", tyv)
                        ysc_tok[sl] = kb.emit(G, lambda E_, e=e, gg=gg, sl=sl: E_.indirect_dma_start(
                            out=moe_out, out_offset=IOA(sidx[:, 8 * gg + e:8 * gg + e + 1], 0),
                            in_=ex["ybuf"][:, 1024 * sl:1024 * (sl + 1)], in_offset=None,
                            bounds_check=breg_of(E_), oob_is_err=False,
                            compute_op=ALU.add), inc="sSC", by=16)
                    scat_prev = kb.val("sSC")
            return scat_prev

        def moe_readback_add(t_scat, ex):
            kb.wait(SY, "sSC", t_scat)
            rtoks = []
            for j in range(NT):
                sl = j % 2
                rtoks.append(dma(SY, "sRA" if sl == 0 else "sRB",
                                 ex["xg_tm"][:, 1024 * sl:1024 * (sl + 1)],
                                 moe_out[128 * j:128 * (j + 1), :]))
                kb.wait(V, "sRA" if sl == 0 else "sRB", rtoks[-1])
                tad = kb.emit(V, lambda E_, j=j, sl=sl: E_.tensor_add(
                    out=stream[:, D * j:D * (j + 1)], in0=stream[:, D * j:D * (j + 1)],
                    in1=ex["xg_tm"][:, 1024 * sl:1024 * (sl + 1)]), inc="cV")
                kb.wait(SY, "cV", tad)
            return kb.val("cV")

        def phase_l1_attn(t_xfm2, l1):
            t_a = dma(SY, "sL1", l1["asb"][:], A_m)
            ld_tok = [None, None]
            for k in range(ND):
                hh = k % 2
                if ld_tok[hh] is not None:
                    kb.wait(SY, "cV", ld_tok[hh])
                t_w = dma(SY, "sWA" if hh == 0 else "sWB",
                          wstg[:, 1024 * hh:1024 * hh + 128], Bm_m[128 * k:128 * (k + 1), :])
                kb.wait(V, "sWA" if hh == 0 else "sWB", t_w)
                ld_tok[hh] = kb.emit(V, lambda E_, k=k, hh=hh: E_.tensor_copy(
                    out=l1["bmbf"][:, 128 * k:128 * (k + 1)],
                    in_=wstg[:, 1024 * hh:1024 * hh + 128]), inc="cV")
            for cb in range(ND):
                hh = cb % 2
                kb.wait(SY, "cV" if ld_tok[hh] is None else "cV", kb.val("cV"))
                t_w = dma(SY, "sWA" if hh == 0 else "sWB",
                          wstg[:, 1024 * hh:1024 * hh + 128], Wctx[128 * cb:128 * (cb + 1), :])
                kb.wait(TE, "sWA" if hh == 0 else "sWB", t_w)
                tt = kb.emit(TE, lambda E_, hh=hh: E_.transpose(
                    pb[0][:, 0:128], wstg[:, 1024 * hh:1024 * hh + 128], id128f[:]), inc="cT")
                kb.wait(V, "cT", tt)
                tcx = kb.emit(V, lambda E_, cb=cb: E_.tensor_copy(
                    out=l1["wctxT"][:, 128 * cb:128 * (cb + 1)], in_=pb[0][:, 0:128]), inc="cV")
                kb.wait(TE, "cV", tcx)
                kb.wait(SY, "cT", kb.val("cT"))
            t_bm = kb.val("cV")

            kb.wait(TE, "cV", max(t_xfm2, t_bm))
            for tc in range(4):
                b = pb[4 + tc % 2]
                tmx = None
                for k in range(ND):
                    tmx = kb.emit(TE, lambda E_, tc=tc, k=k, b=b: E_.matmul(
                        b[:, 0:512], l1["bmbf"][:, 128 * k:128 * (k + 1)],
                        xfm[:, T * k + 512 * tc:T * k + 512 * (tc + 1)],
                        start=(k == 0), stop=(k == ND - 1)),
                        inc="cT" if k == ND - 1 else None)
                kb.wait(V, "cT", tmx)
                tev = kb.emit(V, lambda E_, tc=tc, b=b: E_.tensor_copy(
                    out=l1["S"][:, 512 * tc:512 * (tc + 1)], in_=b[:, 0:512]), inc="cV")
                kb.wait(TE, "cV", tev)

            kb.wait(TE, "sL1", t_a)
            for r in range(11):
                sh = 1 << r
                bounds = []
                hi = T
                while hi > sh:
                    lo = max(sh, hi - 512)
                    bounds.append((lo, hi))
                    hi = lo
                tadd = None
                for ci, (lo, hi) in enumerate(bounds):
                    w = hi - lo
                    b = pb[ci % 4]
                    tsc_ = kb.emit(TE, lambda E_, lo=lo, w=w, sh=sh, b=b: E_.matmul(
                        b[:, 0:w], l1["apow"][:], l1["S"][:, lo - sh:lo - sh + w],
                        start=True, stop=True), inc="cT")
                    kb.wait(V, "cT", tsc_)
                    tadd = kb.emit(V, lambda E_, lo=lo, w=w, b=b: E_.tensor_add(
                        out=l1["S"][:, lo:lo + w], in0=l1["S"][:, lo:lo + w],
                        in1=b[:, 0:w]), inc="cV")
                    kb.wait(TE, "cV", tadd)
                if r < 10:
                    tt = kb.emit(TE, lambda E_: E_.transpose(pb[4][:, 0:128], l1["apow"][:], id128f[:]), inc="cT")
                    kb.wait(V, "cT", tt)
                    tev = kb.emit(V, lambda E_: E_.tensor_copy(out=l1["apT"][:], in_=pb[4][:, 0:128]), inc="cV")
                    kb.wait(TE, "cV", tev)
                    tsq = kb.emit(TE, lambda E_: E_.matmul(pb[5][:, 0:128], l1["apT"][:], l1["apow"][:],
                            start=True, stop=True), inc="cT")
                    kb.wait(V, "cT", tsq)
                    tev2 = kb.emit(V, lambda E_: E_.tensor_copy(out=l1["apow"][:], in_=pb[5][:, 0:128]), inc="cV")
                    kb.wait(TE, "cV", tev2)

            tsb_ = kb.emit(V, lambda E_: E_.tensor_copy(out=l1["S_bf"][:], in_=l1["S"][:]), inc="cV")
            kb.wait(TE, "cV", tsb_)
            for g in range(NT):
                for n in range(2):
                    b = pb[6 + n]
                    tmc = kb.emit(TE, lambda E_, g=g, n=n, b=b: E_.matmul(
                        b[:, 0:512], l1["S_bf"][:, 128 * g:128 * (g + 1)],
                        l1["wctxT"][:, 512 * n:512 * (n + 1)],
                        start=True, stop=True), inc="cT")
                    kb.wait(V, "cT", tmc)
                    tav = kb.emit(V, lambda E_, g=g, n=n, b=b: E_.tensor_add(
                        out=stream[:, D * g + 512 * n:D * g + 512 * (n + 1)],
                        in0=stream[:, D * g + 512 * n:D * g + 512 * (n + 1)],
                        in1=b[:, 0:512]), inc="cV")
                    kb.wait(TE, "cV", tav)
            return kb.val("cV")

        def phase_l1_shared(t_xfm3, sh):
            w1_ld = [None, None]
            w1bf_free = None
            for tc in range(4):
                kb.wait(SY, "cV", kb.val("cV"))
                for half in range(2):
                    if w1bf_free is not None:
                        kb.wait(V, "cT", w1bf_free)
                    cast_toks = []
                    for k in range(ND):
                        hh = k % 2
                        if w1_ld[hh] is not None:
                            kb.wait(SY, "cV", w1_ld[hh])
                        dma(SY, "sWA" if hh == 0 else "sWB",
                            wstg[:, 1024 * hh:1024 * (hh + 1)],
                            sh_up[128 * k:128 * (k + 1), 2048 * half:2048 * half + 1024])
                        t_w2 = dma(SY, "sWA" if hh == 0 else "sWB",
                                   sh["w1stg2"][:, 1024 * hh:1024 * (hh + 1)],
                                   sh_up[128 * k:128 * (k + 1), 2048 * half + 1024:2048 * half + 2048])
                        kb.wait(V, "sWA" if hh == 0 else "sWB", t_w2)
                        kb.emit(V, lambda E_, k=k, hh=hh: E_.tensor_copy(
                            out=sh["w1bf"][:, 2048 * k:2048 * k + 1024],
                            in_=wstg[:, 1024 * hh:1024 * (hh + 1)]))
                        tc2 = kb.emit(V, lambda E_, k=k, hh=hh: E_.tensor_copy(
                            out=sh["w1bf"][:, 2048 * k + 1024:2048 * k + 2048],
                            in_=sh["w1stg2"][:, 1024 * hh:1024 * (hh + 1)]), inc="cV")
                        w1_ld[hh] = tc2
                        cast_toks.append(tc2)
                    if tc == 0 and half == 0:
                        kb.wait(TE, "cV", t_xfm3)
                    kb.wait(TE, "cV", cast_toks[-1])
                    for hc in range(16):
                        b = pb[2 + hc % 2]
                        tmu = None
                        for k in range(ND):
                            tmu = kb.emit(TE, lambda E_, tc=tc, hc=hc, k=k, b=b: E_.matmul(
                                b[:, 0:512],
                                sh["w1bf"][:, 2048 * k + 128 * hc:2048 * k + 128 * (hc + 1)],
                                xfm[:, T * k + 512 * tc:T * k + 512 * (tc + 1)],
                                start=(k == 0), stop=(k == ND - 1)),
                                inc="cT" if k == ND - 1 else None)
                        kb.wait(SC, "cT", tmu)
                        hcg = 16 * half + hc
                        tge = kb.emit(SC, lambda E_, hcg=hcg, b=b: E_.activation(
                            out=sh["hbuf"][:, 512 * hcg:512 * (hcg + 1)], in_=b[:, 0:512],
                            func=AF.Gelu, bias=0.0, scale=1.0), inc="cS")
                        kb.wait(TE, "cS", tge)
                    w1bf_free = kb.val("cT")
                kb.wait(TE, "cS", kb.val("cS"))
                kb.wait(SY, "cV", kb.val("cV"))
                w2_ld = [None, None]
                w2_mm = [None, None]
                tdn = None
                for hc in range(NH):
                    hh = hc % 2
                    if w2_ld[hh] is not None:
                        kb.wait(SY, "cV", w2_ld[hh])
                    t_w = dma(SY, "sW2A" if hh == 0 else "sW2B",
                              wstg[:, 1024 * hh:1024 * (hh + 1)],
                              sh_dn[128 * hc:128 * (hc + 1), :])
                    kb.wait(V, "sW2A" if hh == 0 else "sW2B", t_w)
                    if w2_mm[hh] is not None:
                        kb.wait(V, "cT", w2_mm[hh])
                    tcw = kb.emit(V, lambda E_, hh=hh: E_.tensor_copy(
                        out=sh["w2bf"][:, 1024 * hh:1024 * (hh + 1)],
                        in_=wstg[:, 1024 * hh:1024 * (hh + 1)]), inc="cV")
                    w2_ld[hh] = tcw
                    kb.wait(TE, "cV", tcw)
                    for m in range(4):
                        for n in range(2):
                            b = pb[m * 2 + n]
                            tdn = kb.emit(TE, lambda E_, hc=hc, m=m, n=n, hh=hh, b=b: E_.matmul(
                                b[:, 0:512],
                                sh["hbuf"][:, 512 * hc + 128 * m:512 * hc + 128 * (m + 1)],
                                sh["w2bf"][:, 1024 * hh + 512 * n:1024 * hh + 512 * (n + 1)],
                                start=(hc == 0), stop=(hc == NH - 1)),
                                inc="cT" if (m == 3 and n == 1) else None)
                    w2_mm[hh] = tdn
                kb.wait(V, "cT", tdn)
                for m in range(4):
                    g = 4 * tc + m
                    tvv = None
                    for n in range(2):
                        b = pb[m * 2 + n]
                        d0 = D * g + 512 * n
                        tvv = kb.emit(V, lambda E_, b=b, d0=d0: E_.scalar_tensor_tensor(
                            out=stream[:, d0:d0 + 512], in0=stream[:, d0:d0 + 512],
                            scalar=2.0, in1=b[:, 0:512], op0=ALU.mult, op1=ALU.add), inc="cV")
                    kb.wait(TE, "cV", tvv)
            return kb.val("cV")

        def phase_winT(init_toks):
            stg_free = [None, None]
            first_odd = True
            store_tok = [None, None]
            for oc in range(24):
                h_ = oc % 2
                if stg_free[h_] is not None:
                    kb.wait(SY, "cT", stg_free[h_])
                elif h_ == 1 and first_odd:
                    kb.wait(SY, "cV", init_toks["ob2"])
                    first_odd = False
                t_w = dma(SY, "sWA" if h_ == 0 else "sWB",
                          wstg[:, 1024 * h_:1024 * (h_ + 1)],
                          w_in[128 * oc:128 * (oc + 1), :])
                kb.wait(TE, "sWA" if h_ == 0 else "sWB", t_w)
                if store_tok[h_] is not None:
                    kb.wait(V, "sSA" if h_ == 0 else "sSB", store_tok[h_])
                for k in range(ND):
                    b = pb[k % 2]
                    tt = kb.emit(TE, lambda E, k=k, h_=h_, b=b: E.transpose(
                        b[:, 0:128], wstg[:, 1024 * h_ + 128 * k:1024 * h_ + 128 * (k + 1)],
                        id128f[:]), inc="cT")
                    kb.wait(V, "cT", tt)
                    tv = kb.emit(V, lambda E, k=k, h_=h_, b=b: E.tensor_copy(
                        out=wbf[:, 1024 * h_ + 128 * k:1024 * h_ + 128 * (k + 1)],
                        in_=b[:, 0:128]), inc="cV")
                    if k >= 1:
                        kb.wait(TE, "cV", tv)
                stg_free[h_] = kb.val("cT")
                kb.wait(SC, "cV", kb.val("cV"))
                store_tok[h_] = dma(SC, "sSA" if h_ == 0 else "sSB", winT_d[oc],
                                    wbf[:, 1024 * h_:1024 * (h_ + 1)])
            return store_tok, stg_free

        def phase_l0_attn(init_toks, t_xfm):
            kb.wait(V, "sA", 32)
            kb.wait(TE, "cV", max(t_xfm, init_toks["ob2"]))
            winT_store, winT_stg = phase_winT(init_toks)
            kb.wait(SY, "sSA", kb.val("sSA"))
            kb.wait(SY, "sSB", kb.val("sSB"))

            bank_tok = [None] * 4
            wT_tok = [None, None]

            def band_qkv(bq):
                coff = 0
                for oc in range(24):
                    wp = oc % 2
                    if wT_tok[wp] is not None:
                        kb.wait(SY, "cT", wT_tok[wp])
                    t_l = dma(SY, "sLA" if wp == 0 else "sLB",
                              wTsb[:, 1024 * wp:1024 * (wp + 1)], winT_d[oc])
                    kb.wait(TE, "sLA" if wp == 0 else "sLB", t_l)
                    bi = oc % 4
                    b = pb[4 + bi]
                    if bank_tok[bi] is not None:
                        kb.wait(TE, "cS", bank_tok[bi])
                    tm = None
                    if bq < 3:
                        c0 = 512 * bq + 32
                        for k in range(ND):
                            tm = kb.emit(TE, lambda E, k=k, wp=wp, c0=c0, b=b: E.matmul(
                                b[:, 0:512],
                                wTsb[:, 1024 * wp + 128 * k:1024 * wp + 128 * (k + 1)],
                                xfm[:, T * k + c0:T * k + c0 + 512],
                                start=(k == 0), stop=(k == ND - 1)),
                                inc="cT" if k == ND - 1 else None)
                    else:
                        for k in range(ND):
                            kb.emit(TE, lambda E, k=k, wp=wp, b=b: E.matmul(
                                b[:, 0:480],
                                wTsb[:, 1024 * wp + 128 * k:1024 * wp + 128 * (k + 1)],
                                xfm[:, T * k + 1568:T * k + 2048],
                                start=(k == 0), stop=False))
                        for k in range(ND):
                            tm = kb.emit(TE, lambda E, k=k, wp=wp, b=b: E.matmul(
                                b[:, 480:512],
                                wTsb[:, 1024 * wp + 128 * k:1024 * wp + 128 * (k + 1)],
                                xfm[:, T * k:T * k + 32],
                                start=False, stop=(k == ND - 1)),
                                inc="cT" if k == ND - 1 else None)
                    wT_tok[wp] = tm
                    kb.wait(SC, "cT", tm)
                    bank_tok[bi] = kb.emit(SC, lambda E, oc=oc, coff=coff, b=b: E.activation(
                        out=qkvb[:, coff + 512 * oc:coff + 512 * (oc + 1)],
                        in_=b[:, 0:512], func=AF.Identity,
                        bias=bqkv[:, oc:oc + 1], scale=1.0), inc="cS")
                return kb.val("cS")

            def band_windows(bq, t_qkv):
                coff = 0
                kb.wait(V, "cS", t_qkv)
                kb.wait(TE, "cS", t_qkv)
                for j in range(4 * bq, 4 * bq + 4):

                    jrel = 128 * (j - 4 * bq)
                    tvt = None
                    for wi in range(2):
                        wrel0 = 64 * (2 * j + wi - 8 * bq)
                        for h in range(HEADS):
                            bv = pb[0][:].bitcast(BF16)
                            tt = kb.emit(TE, lambda E, h=h, coff=coff, wrel0=wrel0, bv=bv: E.transpose(
                                bv[0:64, 0:128],
                                qkvb[:, coff + (16 + h) * 512 + wrel0:coff + (16 + h) * 512 + wrel0 + 64],
                                id128b[:]), inc="cT")
                            kb.wait(V, "cT", tt)
                            tvt = kb.emit(V, lambda E, h=h, wi=wi, bv=bv: E.tensor_copy(
                                out=vt[0:64, 1024 * wi + 128 * h:1024 * wi + 128 * (h + 1)],
                                in_=bv[0:64, 0:128]), inc="cV")
                            kb.wait(TE, "cV", tvt)

                    for wi, w in enumerate((2 * j, 2 * j + 1)):
                        wrel = 64 * (w - 8 * bq)
                        sbk = pb[1 + wi]
                        tg = None
                        for h in range(HEADS):
                            tg = kb.emit(TE, lambda E, h=h, coff=coff, wrel=wrel, b=sbk: E.matmul(
                                b[0:64, 64 * h:64 * (h + 1)],
                                qkvb[:, coff + h * 512 + wrel:coff + h * 512 + wrel + 64],
                                qkvb[:, coff + (8 + h) * 512 + wrel:coff + (8 + h) * 512 + wrel + 64],
                                start=(h == 0), stop=(h == HEADS - 1)),
                                inc="cT" if h == HEADS - 1 else None)
                        kb.wait(V, "cT", tg)
                        doff = 0
                        trm = None
                        for h in range(HEADS):
                            trm = kb.emit(V, lambda E, h=h, b=sbk, doff=doff: E.reduce_max(
                                out=mxc[:, doff + h:doff + h + 1],
                                in_=b[0:64, 64 * h:64 * (h + 1)], axis=AX.X),
                                inc="cV" if h == HEADS - 1 else None)
                        kb.wait(V, "cV", trm)
                        tv2 = kb.emit(V, lambda E, doff=doff: E.tensor_scalar_mul(
                            mxc[:, doff:doff + 8], mxc[:, doff:doff + 8], -ISQ), inc="cV")
                        kb.wait(SC, "cV", tv2)
                        aoff = 0
                        tsx = None
                        for h in range(HEADS):
                            tsx = kb.emit(SC, lambda E, h=h, b=sbk, doff=doff, aoff=aoff: E.activation(
                                out=att_n[:, aoff + 64 * h:aoff + 64 * (h + 1)],
                                in_=b[0:64, 64 * h:64 * (h + 1)], func=AF.Exp,
                                bias=mxc[:, doff + h:doff + h + 1], scale=ISQ,
                                accum_out=den[:, doff + h:doff + h + 1]), inc="cS")
                        kb.wait(V, "cS", tsx)
                        kb.wait(TE, "cS", tsx)
                        trc = kb.emit(V, lambda E, doff=doff: E.reciprocal(
                            out=rden[:, doff:doff + 8], in_=den[:, doff:doff + 8]), inc="cV")
                        kb.wait(V, "cV", trc)
                        tvn = None
                        for h in range(HEADS):
                            tvn = kb.emit(V, lambda E, h=h, doff=doff, aoff=aoff: E.tensor_scalar_mul(
                                att_n[:, aoff + 64 * h:aoff + 64 * (h + 1)],
                                att_n[:, aoff + 64 * h:aoff + 64 * (h + 1)],
                                rden[:, doff + h:doff + h + 1]),
                                inc="cV" if h == HEADS - 1 else None)
                        kb.wait(TE, "cV", tvn)
                        tbv = pb[3][:].bitcast(BF16)
                        ttr = None
                        for h in range(HEADS):
                            ttr = kb.emit(TE, lambda E, h=h, aoff=aoff, bv=tbv: E.transpose(
                                bv[0:64, 64 * h:64 * (h + 1)],
                                att_n[:, aoff + 64 * h:aoff + 64 * (h + 1)],
                                id128b[0:64, 0:64]), inc="cT" if h == HEADS - 1 else None)
                        kb.wait(V, "cT", ttr)
                        tve = kb.emit(V, lambda E, aoff=aoff, bv=tbv: E.tensor_copy(
                            out=att_t[:, aoff:aoff + 512], in_=bv[0:64, 0:512]), inc="cV")
                        kb.wait(TE, "cV", tve)
                        obk = pb[4 + wi]
                        tav = None
                        for h in range(HEADS):
                            tav = kb.emit(TE, lambda E, h=h, wi=wi, aoff=aoff, b=obk: E.matmul(
                                b[:, 64 * h:64 * (h + 1)],
                                vt[0:64, 1024 * wi + 128 * h:1024 * wi + 128 * (h + 1)],
                                att_t[:, aoff + 64 * h:aoff + 64 * (h + 1)],
                                start=(h == 0), stop=(h == HEADS - 1)),
                                inc="cT" if h == HEADS - 1 else None)
                        kb.wait(V, "cT", tav)
                        tvo = None
                        for h in range(HEADS):
                            if w < NWIN - 1:
                                tvo = kb.emit(V, lambda E, h=h, w=w, b=obk: E.tensor_copy(
                                    out=ofm[:, T * h + WIN * w + 32:T * h + WIN * w + 96],
                                    in_=b[:, 64 * h:64 * (h + 1)]),
                                    inc="cV" if h == HEADS - 1 else None)
                            else:
                                kb.emit(V, lambda E, h=h, b=obk: E.tensor_copy(
                                    out=ofm[:, T * h + 2016:T * h + 2048],
                                    in_=b[:, 64 * h:64 * h + 32]))
                                tvo = kb.emit(V, lambda E, h=h, b=obk: E.tensor_copy(
                                    out=ofm[:, T * h:T * h + 32],
                                    in_=b[:, 64 * h + 32:64 * (h + 1)]),
                                    inc="cV" if h == HEADS - 1 else None)
                        kb.wait(TE, "cV", tvo)

            stg_free2 = winT_stg[0]
            for cb in range(ND):
                kb.wait(SY, "cT", stg_free2)
                t_w = dma(SY, "sWA", wstg[:, 0:1024], w_out[128 * cb:128 * (cb + 1), :])
                kb.wait(TE, "sWA", t_w)
                for k in range(ND):
                    b = pb[k % 2]
                    tt = kb.emit(TE, lambda E, k=k, b=b: E.transpose(
                        b[:, 0:128], wstg[:, 128 * k:128 * (k + 1)], id128f[:]), inc="cT")
                    kb.wait(V, "cT", tt)
                    tv = kb.emit(V, lambda E, k=k, cb=cb, b=b: E.tensor_copy(
                        out=owT[:, D * k + 128 * cb:D * k + 128 * (cb + 1)],
                        in_=b[:, 0:128]), inc="cV")
                    kb.wait(TE, "cV", tv)
                stg_free2 = kb.val("cT")
            t_owT = kb.val("cV")
            kb.wait(TE, "cV", t_owT)

            for bq in range(4):
                t_q = band_qkv(bq)
                band_windows(bq, t_q)

            kb.wait(TE, "cV", init_toks["ob2"])
            for g in range(NT):
                for n in range(2):
                    d0 = D * g + 512 * n
                    b = pb[6 + n]
                    tm = None
                    for k in range(ND):
                        c0 = T * k + 128 * g
                        tm = kb.emit(TE, lambda E, c0=c0, n=n, k=k, b=b: E.matmul(
                            b[:, 0:512], ofm[:, c0:c0 + 128],
                            owT[:, D * k + 512 * n:D * k + 512 * (n + 1)],
                            start=(k == 0), stop=(k == ND - 1)),
                            inc="cT" if k == ND - 1 else None)
                    kb.wait(V, "cT", tm)
                    kb.emit(V, lambda E, b=b, d0=d0: E.scalar_tensor_tensor(
                        out=stream[:, d0:d0 + 512], in0=b[:, 0:512],
                        scalar=2.0, in1=stream[:, d0:d0 + 512],
                        op0=ALU.mult, op1=ALU.add))
                    tv = kb.emit(V, lambda E, d0=d0, n=n: E.tensor_add(
                        out=stream[:, d0:d0 + 512], in0=stream[:, d0:d0 + 512],
                        in1=outb2[:, 512 * n:512 * (n + 1)]), inc="cV")
                    kb.wait(TE, "cV", tv)
            return kb.val("cV")

        a_ctx = ExitStack()
        bqkv = a_ctx.enter_context(nc.sbuf_tensor("bqkv", [128, 24], F32))
        outb2 = a_ctx.enter_context(nc.sbuf_tensor("outb2", [128, D], F32))
        wbf = a_ctx.enter_context(nc.sbuf_tensor("wbf", [128, 2 * 1024], BF16))
        qkvb = a_ctx.enter_context(nc.sbuf_tensor("qkvb", [128, 24 * 512], BF16))
        wTsb = a_ctx.enter_context(nc.sbuf_tensor("wTsb", [128, 2 * ND * 128], BF16))
        vt = a_ctx.enter_context(nc.sbuf_tensor("vt", [64, 2 * HEADS * 128], BF16))
        att_n = a_ctx.enter_context(nc.sbuf_tensor("att_n", [64, HEADS * 64], BF16))
        att_t = a_ctx.enter_context(nc.sbuf_tensor("att_t", [64, HEADS * 64], BF16))
        den = a_ctx.enter_context(nc.sbuf_tensor("den", [64, HEADS], F32))
        rden = a_ctx.enter_context(nc.sbuf_tensor("rden", [64, HEADS], F32))
        mxc = a_ctx.enter_context(nc.sbuf_tensor("mxc", [64, HEADS], F32))
        ofm = a_ctx.enter_context(nc.sbuf_tensor("ofm", [128, HEADS * T], BF16))
        owT = a_ctx.enter_context(nc.sbuf_tensor("owT", [128, ND * D], BF16))
        init_toks = phase_init()
        t_xfm = phase_input(init_toks)
        phase_l0_attn(init_toks, t_xfm)
        t_ln = ln_all()
        barrier()
        a_ctx.close()

        _selc = [0]

        def sel_scope(sctx):
            _selc[0] += 1
            pfx = f"s{_selc[0]}_"
            return {
                "probs": sctx.enter_context(nc.sbuf_tensor(pfx + "probs", [128, 128], F32)),
                "pmax": sctx.enter_context(nc.sbuf_tensor(pfx + "pmax", [128, 16], F32)),
                "psum_r": sctx.enter_context(nc.sbuf_tensor(pfx + "psum_r", [128, 16], F32)),
                "mx8s": sctx.enter_context(nc.sbuf_tensor(pfx + "mx8s", [128, 128], F32)),
                "thr": sctx.enter_context(nc.sbuf_tensor(pfx + "thr", [1, 16], F32)),
                "thr_rep": sctx.enter_context(nc.sbuf_tensor(pfx + "thr_rep", [128, 8], F32)),
                "maskb": sctx.enter_context(nc.sbuf_tensor(pfx + "maskb", [128, 128], F32)),
                "gatesb": sctx.enter_context(nc.sbuf_tensor(pfx + "gatesb", [128, 128], F32)),
                "m_em": sctx.enter_context(nc.sbuf_tensor(pfx + "m_em", [8, T], F32)),
                "pos_em": sctx.enter_context(nc.sbuf_tensor(pfx + "pos_em", [8, T], F32)),
                "pos16": sctx.enter_context(nc.sbuf_tensor(pfx + "pos16", [8, T], I16)),
                "g_em": sctx.enter_context(nc.sbuf_tensor(pfx + "g_em", [8, T], BF16)),
                "tgt16": sctx.enter_context(nc.sbuf_tensor(pfx + "tgt16", [128, T], I16)),
                "gsp": sctx.enter_context(nc.sbuf_tensor(pfx + "gsp", [128, T], BF16)),
                "lst16": sctx.enter_context(nc.sbuf_tensor(pfx + "lst16", [128, T], I16)),
                "glst": sctx.enter_context(nc.sbuf_tensor(pfx + "glst", [128, T], BF16)),
                "lstC16": sctx.enter_context(nc.sbuf_tensor(pfx + "lstC16", [8, CAP1], I16)),
                "lstC": sctx.enter_context(nc.sbuf_tensor(pfx + "lstC", [8, CAP1], F32)),
                "glstC": sctx.enter_context(nc.sbuf_tensor(pfx + "glstC", [8, CAP1], BF16)),
                "lcolf": sctx.enter_context(nc.sbuf_tensor(pfx + "lcolf", [128, 8], F32)),
                "t2f": sctx.enter_context(nc.sbuf_tensor(pfx + "t2f", [128, 8], F32)),
                "iseq": sctx.enter_context(nc.sbuf_tensor(pfx + "iseq", [128, 8], F32)),
                "t1f": sctx.enter_context(nc.sbuf_tensor(pfx + "t1f", [128, 8], F32)),
            }

        _exc = [0]

        def ex_scope(ectx):
            _exc[0] += 1
            epfx = f"e{_exc[0]}_"
            d = {
                "xg_tm": ectx.enter_context(nc.sbuf_tensor(epfx + "xg_tm", [128, 2 * D], F32)),
                "xg_fm": ectx.enter_context(nc.sbuf_tensor(epfx + "xg_fm", [128, ND * 512], BF16)),
                "hbuf": ectx.enter_context(nc.sbuf_tensor(epfx + "hbuf", [128, NH * 512], BF16)),
                "w1bf": ectx.enter_context(nc.sbuf_tensor(epfx + "w1bf", [128, 8 * 2048], BF16)),
                "w1stg2": ectx.enter_context(nc.sbuf_tensor(epfx + "w1stg2", [128, 2 * 1024], F32)),
                "w2bf": ectx.enter_context(nc.sbuf_tensor(epfx + "w2bf", [128, 2 * 1024], BF16)),
            }
            d["ybuf"] = d["w1stg2"]
            return d

        if STAGE >= 2:
            t_g0 = dma(SY, "sL1", wstg[:, 0:64].rearrange("p (k e) -> p k e", e=8),
                       gate_w.rearrange("(k p) e -> p k e", p=128), slow=True)
            t_xfm1, t_gmm1 = build_xfm(gate_all=t_ln, gate_mm=True, gate_tok=t_g0)
            kb.wait(SY, "cV", t_ln)
            for j in range(NT):
                dma(SY, "sMS", moe_src[128 * j:128 * (j + 1), :], stream[:, D * j:D * (j + 1)])
            s_ctx = ExitStack()
            sel = sel_scope(s_ctx)
            moe_selection(gate_w, True, CAP0, t_gmm1, sel)
            barrier()
            s_ctx.close()
            e_ctx = ExitStack()
            ex = ex_scope(e_ctx)
            t_zero = moe_zero_out()
            t_scat = moe_experts(w1, w2, CAP0, ex, t_zero)
            if DBG_MOE:
                kb.wait(SY, "sSC", t_scat)
                for j in range(NT):
                    dma(SY, "sO", out_t[128 * j:128 * (j + 1), :],
                        moe_out[128 * j:128 * (j + 1), :])
                gxf = ex["xg_tm"]
                kb.emit(V, lambda E: E.tensor_copy(out=gxf[:, 0:64], in_=gidx[:]))
                kb.emit(V, lambda E: E.tensor_copy(out=gxf[:, 64:128], in_=sidx[:]))
                tdg = kb.emit(V, lambda E: E.tensor_copy(out=gxf[:, 128:192], in_=gcol[:]), inc="cV")
                kb.wait(SC, "cV", tdg)
                dma(SC, "sO", dbg_i[0], gxf[:, 0:64])
                dma(SC, "sO", dbg_i[1], gxf[:, 64:128])
                dma(SC, "sO", dbg_i[2], gxf[:, 128:192])
                kb.wait(SY, "sO", kb.val("sO"))
            moe_readback_add(t_scat, ex)
            t_ln = ln_all()
            barrier()
            e_ctx.close()

        if STAGE >= 3:
            t_xfm2 = build_xfm(gate_all=t_ln)[0]
            l1_ctx = ExitStack()
            l1 = {
                "asb": l1_ctx.enter_context(nc.sbuf_tensor("asb", [128, 128], F32)),
                "apow": None,
                "apT": l1_ctx.enter_context(nc.sbuf_tensor("apT", [128, 128], F32)),
                "bmbf": l1_ctx.enter_context(nc.sbuf_tensor("bmbf", [128, ND * 128], BF16)),
                "wctxT": l1_ctx.enter_context(nc.sbuf_tensor("wctxT", [128, D], BF16)),
                "S": l1_ctx.enter_context(nc.sbuf_tensor("S_sb", [128, T], F32)),
                "S_bf": l1_ctx.enter_context(nc.sbuf_tensor("S_bf", [128, T], BF16)),
            }
            l1["apow"] = l1["asb"]
            phase_l1_attn(t_xfm2, l1)
            t_ln = ln_all()
            barrier()
            l1_ctx.close()

        if STAGE >= 4:
            t_g1 = dma(SY, "sL1", wstg[:, 0:64].rearrange("p (k e) -> p k e", e=8),
                       rt_w.rearrange("(k p) e -> p k e", p=128), slow=True)
            t_xfm3, t_gmm3 = build_xfm(gate_all=t_ln, gate_mm=True, gate_tok=t_g1)
            kb.wait(SY, "cV", t_ln)
            for j in range(NT):
                dma(SY, "sMS", moe_src[128 * j:128 * (j + 1), :], stream[:, D * j:D * (j + 1)])
            s_ctx2 = ExitStack()
            sel2 = sel_scope(s_ctx2)
            moe_selection(rt_w, False, CAP1, t_gmm3, sel2)
            barrier()
            s_ctx2.close()
            sh_ctx = ExitStack()
            sh = {
                "w1bf": sh_ctx.enter_context(nc.sbuf_tensor("shw1bf", [128, 8 * 2048], BF16)),
                "w1stg2": sh_ctx.enter_context(nc.sbuf_tensor("shw1stg2", [128, 2 * 1024], F32)),
                "w2bf": sh_ctx.enter_context(nc.sbuf_tensor("shw2bf", [128, 2 * 1024], BF16)),
                "hbuf": sh_ctx.enter_context(nc.sbuf_tensor("shhbuf", [128, NH * 512], BF16)),
            }
            kb.wait(V, "sMS", kb.val("sMS"))
            phase_l1_shared(t_xfm3, sh)
            barrier()
            sh_ctx.close()
            e_ctx2 = ExitStack()
            ex2 = ex_scope(e_ctx2)
            t_zero2 = moe_zero_out()
            t_scat2 = moe_experts(r_up, r_dn, CAP1, ex2, t_zero2)
            moe_readback_add(t_scat2, ex2)
            t_ln = ln_all()
            barrier()
            e_ctx2.close()

        tv = kb.val("cV")
        kb.wait(SC, "cV", tv)
        for j in range(NT):
            dma(SC, "sO", out_t[128 * j:128 * (j + 1), :], stream[:, D * j:D * (j + 1)])
        kb.wait(SY, "sO", kb.val("sO"))

        block = ctx.enter_context(nc.Block())

        @block.sync
        def _(e):
            for fn in kb.streams["sync"]:
                fn(e)

        @block.vector
        def _(e):
            for fn in kb.streams["vector"]:
                fn(e)

        @block.scalar
        def _(e):
            for fn in kb.streams["scalar"]:
                fn(e)

        @block.tensor
        def _(e):
            for fn in kb.streams["tensor"]:
                fn(e)

        @block.gpsimd
        def _(e):
            for fn in kb.streams["gpsimd"]:
                fn(e)

    return nc


def kernel(**inputs):
    nc = build_core_program()
    mybir.codegen_inst_isa_subclasses(nc)
    names = {
        "in_proj_w": "L0_attn_in_proj_w", "in_proj_b": "L0_attn_in_proj_b",
        "out_w": "L0_attn_out_w", "out_b": "L0_attn_out_b",
        "gate_w": "L0_ffn_gate_w", "w1": "L0_ffn_w1", "w2": "L0_ffn_w2",
        "A": "L1_attn_A", "Bm": "L1_attn_Bm", "W_ctx": "L1_attn_W_ctx",
        "shared_up": "L1_ffn_shared_up", "shared_down": "L1_ffn_shared_down",
        "routed_up": "L1_ffn_routed_up", "routed_down": "L1_ffn_routed_down",
        "router_w": "L1_ffn_router_w",
    }
    x = np.ascontiguousarray(np.asarray(inputs["L0_attn_x"], dtype=np.float32))
    common = {}
    for k, src in names.items():
        a = np.ascontiguousarray(np.asarray(inputs[src], dtype=np.float32))
        if k in ("shared_up", "shared_down"):
            a = np.ascontiguousarray(a[0])
        common[k] = a
    in_maps = [dict(common, x=np.ascontiguousarray(x[c])) for c in range(8)]
    res = run_bass_kernel_spmd(nc, in_maps, core_ids=list(range(8)))
    return np.stack([res.results[c]["out"] for c in range(8)], axis=0)
